# version 04 (final)
# baseline (speedup 1.0000x reference)
import numpy as np
import ml_dtypes
import concourse.bass as bass
import concourse.mybir as mybir
from concourse.bass_utils import run_bass_kernel_spmd

F32 = mybir.dt.float32
BF16 = mybir.dt.bfloat16
I32 = mybir.dt.int32
AF = mybir.ActivationFunctionType
ALU = mybir.AluOpType
AXX = mybir.AxisListType.X

D = 2048
H = 16
NCHIP = 8
CHUNK = 1024
NQ = 2
QROWS = NQ * CHUNK
NVIS = 9
SCALE = 1.0 / (128.0 ** 0.25)
EPS = 1e-5
EXP_SHIFT = -3.0

N_PROD = NVIS
G_SLOTS = 8
SHARED_G = False

QPAIRS = [(0, 7), (1, 6), (2, 5), (3, 4)]

_DEBUG = False
H_FLASH = H
PHASE_LIMIT = 99
PG_ON = (True, True, True)
PG_N = 99
PG_SKIP = 0
PG_IDX_OVERRIDE = None


class _StopEmission(Exception):
    pass



def _core_plan(c):
    b = 0 if c < 4 else 1
    qa, qb = QPAIRS[c % 4]
    visits = [(0, qa), (1, qb)]
    visits += [(0, s) for s in range(qa)]
    visits += [(1, s) for s in range(qb)]
    assert len(visits) == NVIS
    prod = [s for (qc, s) in visits]
    return b, qa, qb, prod, visits


TAB_GK = 0
TAB_GV = TAB_GK + NVIS
TAB_GQ = TAB_GV + NVIS
TAB_FD = TAB_GQ + NVIS
TAB_LEN = TAB_FD + NVIS


def _make_tab(c):
    b, qa, qb, prod, visits = _core_plan(c)
    pos = {}
    for i, s in enumerate(prod):
        pos.setdefault(s, i)
    t = []
    t += [pos[s] * 128 for (qc, s) in visits]
    t += [pos[s] * CHUNK for (qc, s) in visits]
    t += [qc * CHUNK for (qc, s) in visits]
    t += [qc * 1032 for (qc, s) in visits]
    return np.asarray(t, dtype=np.int32)



class Prog:
    ENGINES = ("sync", "scalar", "vector", "tensor", "gpsimd")

    def __init__(self):
        self.items = {e: [] for e in self.ENGINES}
        self.count = {}
        self.sem_names = []
        self.stopped = False

    def sem(self, name):
        if name not in self.count:
            self.count[name] = 0
            self.sem_names.append(name)
        return name

    def emit(self, eng, fn, waits=(), inc=None, n=None):
        if self.stopped:
            return None
        tok = None
        if inc is not None:
            self.sem(inc)
            self.count[inc] += n
            tok = (inc, self.count[inc])
        self.items[eng].append((fn, [w for w in waits if w is not None], inc, n))
        return tok

    def tok(self, name):
        return (self.sem(name), self.count[name])

    def barrier(self, tag):
        if self.stopped:
            return None
        name = self.sem("bar")
        drain = [(s, v) for s, v in self.count.items() if v > 0 and s != "bar"]
        for e in self.ENGINES:
            self.emit(e, lambda eng: eng.nop(), waits=drain, inc="bar", n=1)
        val = self.count["bar"]
        for e in self.ENGINES:
            self.emit(e, lambda eng: None, waits=[("bar", val)])
        return ("bar", val)


def _mk_body(prog, sems, eng_name):
    hw = {}

    def run(eng):
        for fn, waits, inc, n in prog.items[eng_name]:
            for (s, v) in waits:
                if hw.get(s, -1) < v:
                    eng.wait_ge(sems[s], v)
                    hw[s] = v
            ins = fn(eng)
            if inc is not None:
                if ins is None:
                    ins = eng.nop()
                ins.then_inc(sems[inc], n)
    return run



def build_nc():
    nc = bass.Bass("TRN2", target_bir_lowering=False, dynamic_dma_scratch_size=16384)

    xin = nc.dram_tensor("xin", [N_PROD * CHUNK, D], F32, kind="ExternalInput").ap()
    tab = nc.dram_tensor("tab", [TAB_LEN], I32, kind="ExternalInput").ap()
    itab = nc.dram_tensor("itab", [128, 5 * NVIS], I32, kind="ExternalInput").ap()
    wts = {n: nc.dram_tensor(n, [D, D], F32, kind="ExternalInput").ap()
           for n in ("wq", "wk", "wv", "wo", "wf")}
    biases = nc.dram_tensor("biases", [4 * D], F32, kind="ExternalInput").ap()
    lnwb = nc.dram_tensor("lnwb", [2, D], F32, kind="ExternalInput").ap()
    xout = nc.dram_tensor("xout", [QROWS, D], F32, kind="ExternalOutput").ap()
    dbg = {}
    if _DEBUG:
        for nm, shp in (("x1", [CHUNK, D]), ("xn2", [CHUNK, D]), ("wp", [D, D]),
                        ("wpe", [128, 16 * D]),
                        ("attn", [CHUNK, D]), ("x2", [128, D]), ("kk", [128, D]),
                        ("vv", [128, 2064]), ("ds", [128, 1032]),
                        ("vb", [128, 1032]), ("pb", [128, 3072]),
                        ("kq", [128, 2048]), ("kd", [128, 1024]),
                        ("qd", [1024, 128])):
            dbg[nm] = nc.dram_tensor("dbg_" + nm, shp, F32, kind="ExternalOutput").ap()

    gkw = {"addr_space": "Shared"} if SHARED_G else {}
    pk_raw = nc.dram_tensor("pk_raw", [N_PROD * 128, H * CHUNK], BF16, kind="Internal", **gkw).ap()
    pv_raw = nc.dram_tensor("pv_raw", [N_PROD * CHUNK, H * 129], BF16, kind="Internal", **gkw).ap()
    pf_q = nc.dram_tensor("pf_q", [NVIS * CHUNK, D], BF16, kind="Internal").ap()
    wbf = {n: nc.dram_tensor(n + "_bf", [D, D], BF16, kind="Internal").ap()
           for n in ("wq", "wk", "wv", "wo")}
    wp_sp = nc.dram_tensor("wp_sp", [D, D], BF16, kind="Internal").ap()
    xaff_sp = nc.dram_tensor("xaff_sp", [CHUNK, D], BF16, kind="Internal").ap()
    xn2_sp = nc.dram_tensor("xn2_sp", [N_PROD * CHUNK, D], BF16, kind="Internal").ap()
    x1_sp = nc.dram_tensor("x1_sp", [QROWS, D], F32, kind="Internal").ap()
    x2_sp = nc.dram_tensor("x2_sp", [QROWS, D], F32, kind="Internal").ap()
    attn_sp = nc.dram_tensor("attn_sp", [QROWS, D], BF16, kind="Internal").ap()
    xaff2_sp = nc.dram_tensor("xaff2_sp", [QROWS, D], BF16, kind="Internal").ap()
    scal_sp = nc.dram_tensor("scal_sp", [256], F32, kind="Internal").ap()
    q_sp = nc.dram_tensor("q_sp", [QROWS, D], BF16, kind="Internal").ap()
    dacc = nc.dram_tensor("dacc", [NQ * 128, 1032], F32, kind="Internal").ap()

    ones_row = nc.inline_tensor(np.ones((1, 2048), dtype=ml_dtypes.bfloat16), "c_ones")
    tri_np = (np.arange(128)[:, None] <= np.arange(128)[None, :]).astype(ml_dtypes.bfloat16)
    tri_h = nc.inline_tensor(tri_np, "c_tri")
    CONSTS = [EXP_SHIFT, EPS, -0.5, 0.5, 1.0 / D, SCALE, 0.0, 1.0]
    C_SHIFT, C_EPS, C_NHALF, C_HALF, C_INVD, C_SCALE, C_ZERO, C_ONE = range(8)
    cst_h = nc.inline_tensor(np.tile(np.array(CONSTS, dtype=np.float32)[None, :], (128, 1)), "c_cst")

    prog = Prog()
    S = prog.sem

    guards = []

    def sballoc(name, shape, dtype):
        g = nc.sbuf_tensor(name, shape, dtype)
        t = g.__enter__()
        guards.append((name, g))
        return t

    def psalloc(name, shape):
        g = nc.psum_tensor(name, shape, F32)
        t = g.__enter__()
        guards.append((name, g))
        return t

    def sbfree(*names):
        for name in names:
            assert guards and guards[-1][0] == name, (name, [g[0] for g in guards[-4:]])
            guards.pop()[1].__exit__(None, None, None)

    def maybe_stop(k):
        if PHASE_LIMIT < k and not prog.stopped:
            for e in Prog.ENGINES:
                for s in list(prog.sem_names):
                    v = prog.count[s]
                    if v > 0:
                        prog.emit(e, lambda eng: None, waits=[(s, v)])
            prog.stopped = True

    regs = {
        "sync": nc.sync.alloc_register("r_sync"),
        "scalar": nc.scalar.alloc_register("r_scal"),
        "vector": nc.vector.alloc_register("r_vec"),
        "tensor": nc.tensor.alloc_register("r_tens"),
        "gpsimd": nc.gpsimd.alloc_register("r_gps"),
    }

    tab_sb = sballoc("tab_sb", [1, TAB_LEN], I32)
    itab_sb = sballoc("itab_sb", [128, 5 * NVIS], I32)
    cst = sballoc("cst", [128, len(CONSTS)], F32)
    for ci, v in enumerate(CONSTS):
        nc.const_aps.register(F32, float(v), cst[:, ci:ci + 1])
    ccol = lambda i: cst[:, i:i + 1]
    ones_sb = sballoc("ones_sb", [1, 2048], BF16)
    tri_sb = sballoc("tri_sb", [128, 128], BF16)
    bias_bf = sballoc("bias_bf", [1, 4 * D], BF16)
    lnw_b = sballoc("lnw_b", [128, D], F32)
    lnb_b = sballoc("lnb_b", [128, D], F32)
    stats = sballoc("stats", [128, 48], F32)
    thr_col = sballoc("thr_col", [128, 1], F32)

    def tabload(eng_name, idx, maxv, add=0):
        def f(e):
            e.reg_load(regs[eng_name], tab_sb[0:1, idx:idx + 1])
            if add:
                e.reg_alu(regs[eng_name], regs[eng_name], add, ALU.add)
            e.get_value_cache().barrier()
            return e.snap(regs[eng_name], donate=True, min_val=add, max_val=maxv + add)
        return f

    class _Emit:
        pass

    t_tab = prog.emit("sync", lambda e: e.dma_start(out=tab_sb[:, :], in_=tab[None, :]), inc="ld0", n=16)
    t_itab = prog.emit("sync", lambda e: e.dma_start(out=itab_sb[:, :], in_=itab[:, :]), inc="ld0", n=16)
    t_cst = prog.emit("sync", lambda e: e.dma_start(out=cst[:, :], in_=cst_h.ap()), inc="ld0", n=16)
    t_ones = prog.emit("sync", lambda e: e.dma_start(out=ones_sb[:, :], in_=ones_row.ap()), inc="ld0", n=16)
    t_tri = prog.emit("sync", lambda e: e.dma_start(out=tri_sb[:, :], in_=tri_h.ap()), inc="ld0", n=16)
    t_lnw = prog.emit("sync", lambda e: e.dma_start(out=lnw_b[:, :], in_=lnwb[0:1, :].partition_broadcast(128)),
                      inc="ld0", n=16)
    t_lnb = prog.emit("sync", lambda e: e.dma_start(out=lnb_b[:, :], in_=lnwb[1:2, :].partition_broadcast(128)),
                      inc="ld0", n=16)
    t_bias = prog.emit("gpsimd", lambda e: e.dma_start(out=bias_bf[:, :], in_=biases[None, :]), inc="cst0", n=16)
    t_wcast = {}
    for n in ("wq", "wk", "wv", "wo"):
        t_wcast[n] = prog.emit("gpsimd", lambda e, n=n: e.dma_start(out=wbf[n][:, :], in_=wts[n][:, :]),
                               inc="cst0", n=16)

    wprime = sballoc("wprime", [128, 16 * D], BF16)
    wslab = sballoc("wslab", [128, D], F32)
    wsg = sballoc("wsg", [128, D], F32)
    wt_ = sballoc("wt_", [128, D], F32)
    abss = sballoc("abss", [128, 16], F32)
    rowred = sballoc("rowred", [1, 128], F32)
    scal1 = sballoc("scal1", [1, 4], F32)

    tA = []
    for s in range(16):
        tl = prog.emit("scalar", lambda e, s=s: e.dma_start(out=wslab[:, :], in_=wts["wf"][s * 128:(s + 1) * 128, :]),
                       waits=[tA[-1] if tA else None], inc="wld", n=16)
        ta = prog.emit("scalar", lambda e, s=s: e.activation(wsg[:, :], wslab[:, :], AF.Abs,
                                                             accum_out=abss[:, s:s + 1]),
                       waits=[tl, t_cst], inc="wabs", n=1)
        tA.append(ta)
    t_r1 = prog.emit("vector", lambda e: e.reduce_sum(stats[:, 0:1], abss[:, :], axis=AXX),
                     waits=[tA[-1]], inc="wqv", n=1)
    t_r2 = prog.emit("gpsimd", lambda e: e.dma_start(out=scal_sp[0:128], in_=stats[:, 0:1]),
                     waits=[t_r1], inc="wqd", n=16)
    t_r3 = prog.emit("gpsimd", lambda e: e.dma_start(out=rowred[:, :], in_=scal_sp[None, 0:128]),
                     waits=[t_r2], inc="wqd", n=16)
    t_r4 = prog.emit("vector", lambda e: e.reduce_sum(scal1[:, 0:1], rowred[:, :], axis=AXX),
                     waits=[t_r3], inc="wqv", n=1)
    t_r5 = prog.emit("vector", lambda e: e.tensor_scalar(scal1[:, 1:2], scal1[:, 0:1], 0.5 / (D * D), None, ALU.mult),
                     waits=[t_r4], inc="wqv", n=1)
    t_r6 = prog.emit("gpsimd", lambda e: e.dma_start(out=scal_sp[128:129], in_=scal1[0:1, 1:2]),
                     waits=[t_r5], inc="wqd", n=16)
    t_r7 = prog.emit("gpsimd", lambda e: e.dma_start(out=thr_col[:, :],
                                                     in_=scal_sp[None, 128:129].partition_broadcast(128)),
                     waits=[t_r6], inc="wqd", n=16)

    t_wp_done = []
    t_wp_sp = []
    for s in range(16):
        tl = prog.emit("scalar", lambda e, s=s: e.dma_start(out=wslab[:, :], in_=wts["wf"][s * 128:(s + 1) * 128, :]),
                       waits=[tA[-1], (t_wp_done[-1] if t_wp_done else None)], inc="wld", n=16)
        ts1 = prog.emit("scalar", lambda e: e.activation(wsg[:, :], wslab[:, :], AF.Sign),
                        waits=[tl, (t_wp_done[-1] if t_wp_done else None)], inc="wqs", n=1)
        tt0 = prog.emit("scalar", lambda e: e.activation(wt_[:, :], wslab[:, :], AF.Abs),
                        waits=[tl, t_r7, (t_wp_done[-1] if t_wp_done else None)], inc="wqs", n=1)
        tt1 = prog.emit("vector", lambda e: e.tensor_scalar(wt_[:, :], wt_[:, :], thr_col[:, :], None,
                                                            ALU.subtract),
                        waits=[tt0, t_r7, (t_wp_done[-1] if t_wp_done else None)], inc="wqv", n=1)
        tt2 = prog.emit("scalar", lambda e: e.activation(wt_[:, :], wt_[:, :], AF.Sign),
                        waits=[tt1], inc="wqs", n=1)
        tt3 = prog.emit("vector", lambda e: e.scalar_tensor_tensor(wt_[:, :], wt_[:, :], 1.0, wsg[:, :],
                                                                   ALU.add, ALU.mult),
                        waits=[tt2, ts1], inc="wqv", n=1)
        tt4 = prog.emit("vector", lambda e, s=s: e.tensor_scalar(wprime[:, s * D:(s + 1) * D], wt_[:, :],
                                                                 0.25, None, ALU.mult),
                        waits=[tt3], inc="wqv", n=1)
        t_wp_done.append(tt4)
        t_wp_sp.append(prog.emit("gpsimd", lambda e, s=s: e.dma_start(out=wp_sp[s * 128:(s + 1) * 128, :],
                                                                      in_=wprime[:, s * D:(s + 1) * D]),
                                 waits=[tt4], inc="wpsp", n=16))
    t_wp_all = t_wp_done[-1]

    prog.barrier("p0q")
    sbfree("scal1", "rowred", "abss", "wt_", "wsg", "wslab")

    xtile = sballoc("xtile", [128, D], F32)
    xn1 = sballoc("xn1", [128, D], F32)
    xaff = sballoc("xaff", [128, D], BF16)
    x1t = sballoc("x1t", [128, D], F32)
    xn2t = sballoc("xn2t", [128, D], BF16)
    xfm = sballoc("xfm", [128, 16 * 512], BF16)
    ps_f = psalloc("ps_f", [128, 2 * 512])

    ln_prev = {}

    def emit_ln(purpose, tile_ap, junk_ap, base, tok_in, junk_guard=None):
        prev = ln_prev.get(purpose)
        b = base
        t1 = prog.emit("vector", lambda e: e.reduce_sum(stats[:, b + 2:b + 3], tile_ap, axis=AXX),
                       waits=[tok_in, prev], inc="lnv", n=1)
        t2 = prog.emit("scalar", lambda e: e.activation(junk_ap, tile_ap, AF.Square,
                                                        accum_out=stats[:, b + 3:b + 4]),
                       waits=[tok_in, prev, junk_guard], inc="lns", n=1)
        t3 = prog.emit("vector", lambda e: e.tensor_scalar(stats[:, b:b + 1], stats[:, b + 2:b + 3],
                                                           1.0 / D, None, ALU.mult),
                       waits=[t1], inc="lnv", n=1)
        t4 = prog.emit("vector", lambda e: e.tensor_scalar(stats[:, b + 4:b + 5], stats[:, b + 3:b + 4],
                                                           1.0 / D, None, ALU.mult),
                       waits=[t2], inc="lnv", n=1)
        t5 = prog.emit("vector", lambda e: e.tensor_tensor(stats[:, b + 5:b + 6], stats[:, b:b + 1],
                                                           stats[:, b:b + 1], ALU.mult),
                       waits=[t3], inc="lnv", n=1)
        t6 = prog.emit("vector", lambda e: e.tensor_tensor(stats[:, b + 4:b + 5], stats[:, b + 4:b + 5],
                                                           stats[:, b + 5:b + 6], ALU.subtract),
                       waits=[t4, t5], inc="lnv", n=1)
        t7 = prog.emit("scalar", lambda e: e.activation(stats[:, b + 5:b + 6], stats[:, b + 4:b + 5],
                                                        AF.Ln, bias=ccol(C_EPS)),
                       waits=[t6], inc="lns", n=1)
        t8 = prog.emit("scalar", lambda e: e.activation(stats[:, b + 1:b + 2], stats[:, b + 5:b + 6],
                                                        AF.Exp, scale=ccol(C_NHALF)),
                       waits=[t7], inc="lns", n=1)
        ln_prev[purpose] = t8
        return t8

    def ln_apply(tile_out, tile_in, base, tok, extra=(), out_tok_name="lnv"):
        return prog.emit("vector", lambda e: e.tensor_scalar(tile_out, tile_in, stats[:, base:base + 1],
                                                             stats[:, base + 1:base + 2],
                                                             ALU.subtract, ALU.mult),
                         waits=[tok] + list(extra), inc=out_tok_name, n=1)

    t_xfm_read = None
    t_xn2_done = None
    evac_bank = {}

    for pc in range(N_PROD):
        t_aff = []
        for tt in range(8):
            row0 = pc * CHUNK + tt * 128
            tx = prog.emit("scalar", lambda e, row0=row0: e.dma_start(out=xtile[:, :], in_=xin[row0:row0 + 128, :]),
                           waits=[t_aff[-1] if t_aff else t_cst, t_ones], inc="xld", n=16)
            tst = emit_ln("ln1", xtile[:, :], xn1[:, :], 0, tx)
            ta1 = ln_apply(xn1[:, :], xtile[:, :], 0, tst)
            tst2 = emit_ln("ln2", xn1[:, :], xtile[:, :], 8, ta1)
            ta2 = ln_apply(xn1[:, :], xn1[:, :], 8, tst2)
            ta3 = prog.emit("vector", lambda e: e.tensor_tensor(xn1[:, :], xn1[:, :], lnw_b[:, :], ALU.mult),
                            waits=[ta2, t_lnw], inc="lnv", n=1)
            ta4 = prog.emit("vector", lambda e: e.tensor_tensor(xaff[:, :], xn1[:, :], lnb_b[:, :], ALU.add),
                            waits=[ta3, t_lnb, (t_aff[-1] if t_aff else None)], inc="lnv", n=1)
            tsp = prog.emit("gpsimd", lambda e, tt=tt: e.dma_start(out=xaff_sp[tt * 128:(tt + 1) * 128, :],
                                                                   in_=xaff[:, :]),
                            waits=[ta4, t_xfm_read], inc="affsp", n=16)
            t_aff.append(tsp)

        for tg in range(2):
            for ic in range(16):
                t_xfm_read = prog.emit("sync", lambda e, ic=ic, tg=tg: e.dma_start(
                    out=xfm[:, ic * 512:(ic + 1) * 512],
                    in_=xaff_sp[tg * 512:(tg + 1) * 512, ic * 128:(ic + 1) * 128], transpose=True),
                    waits=([t_aff[-1], prog.tok("pef")] if ic == 0 else []), inc="tld", n=16)
            t_fm = t_xfm_read
            for tt4 in range(4):
                tt = tg * 4 + tt4
                row0 = pc * CHUNK + tt * 128
                tx2 = prog.emit("gpsimd", lambda e, row0=row0: e.dma_start(out=x1t[:, :], in_=xin[row0:row0 + 128, :]),
                                waits=[evac_bank.get(("x1t", 0)), evac_bank.get(("x1sp", 0))],
                                inc="xld2", n=16)
                for og in range(4):
                    bank = og % 2
                    for ic in range(16):
                        w = []
                        if ic == 0:
                            w = [t_fm, t_wp_all, evac_bank.get(("ps_f", bank))]
                        prog.emit("tensor", lambda e, ic=ic, og=og, tt4=tt4, bank=bank: e.matmul(
                            ps_f[:, bank * 512:(bank + 1) * 512],
                            xfm[:, ic * 512 + tt4 * 128: ic * 512 + tt4 * 128 + 128],
                            wprime[:, ic * D + og * 512: ic * D + (og + 1) * 512],
                            start=(ic == 0), stop=(ic == 15)),
                            waits=w, inc=("pef" if ic == 15 else None), n=(1 if ic == 15 else None))
                    t_mm = prog.tok("pef")
                    tev = prog.emit("vector", lambda e, og=og, bank=bank: e.tensor_tensor(
                        x1t[:, og * 512:(og + 1) * 512], ps_f[:, bank * 512:(bank + 1) * 512],
                        x1t[:, og * 512:(og + 1) * 512], ALU.add),
                        waits=[t_mm, tx2], inc="fev", n=1)
                    evac_bank[("ps_f", bank)] = tev
                t_x1 = evac_bank[("ps_f", (4 - 1) % 2)]
                if pc < NQ:
                    t_x1sp = prog.emit("gpsimd", lambda e, pc=pc, tt=tt: e.dma_start(
                        out=x1_sp[pc * CHUNK + tt * 128: pc * CHUNK + tt * 128 + 128, :], in_=x1t[:, :]),
                        waits=[t_x1], inc="x1sp", n=16)
                    evac_bank[("x1sp", 0)] = t_x1sp
                tst3 = emit_ln("ln3", x1t[:, :], xn1[:, :], 16, t_x1)
                ta5 = ln_apply(xn2t[:, :], x1t[:, :], 16, tst3, extra=[t_xn2_done])
                evac_bank[("x1t", 0)] = ta5
                t_xn2_done = prog.emit("gpsimd", lambda e, pc=pc, tt=tt: e.dma_start(
                    out=xn2_sp[pc * CHUNK + tt * 128: pc * CHUNK + tt * 128 + 128, :], in_=xn2t[:, :]),
                    waits=[ta5], inc="xn2sp", n=16)

    if _DEBUG:
        prog.emit("gpsimd", lambda e: e.dma_start(out=dbg["wpe"][:, :], in_=wprime[:, :]),
                  waits=[prog.tok("pef")], inc="dbgd", n=16)
    maybe_stop(2)
    t_p1a = prog.barrier("p1a")
    sbfree("ps_f", "xfm", "xn2t", "x1t", "xaff", "xn1", "xtile", "wprime")

    wkT = sballoc("wkT", [128, 16 * D], BF16)
    wvT = sballoc("wvT", [128, 16 * D], BF16)
    xfm2 = sballoc("xfm2", [128, 16 * 512], BF16)
    kstage = sballoc("kstage", [128, 16 * 512], BF16)
    vstage = sballoc("vstage", [128, 16 * 129], BF16)
    ps_k = psalloc("ps_k", [128, 2 * 512])
    ps_v = psalloc("ps_v", [128, 2 * 512])

    t_p1a_dmas = [prog.tok("xn2sp"), prog.tok("x1sp"), prog.tok("affsp"), prog.tok("xld2")]
    for ic in range(16):
        t_wkT = prog.emit("sync", lambda e, ic=ic: e.dma_start(
            out=wkT[:, ic * D:(ic + 1) * D], in_=wbf["wk"][:, ic * 128:(ic + 1) * 128], transpose=True),
            waits=([t_wcast["wk"], t_p1a] + t_p1a_dmas if ic == 0 else []), inc="wtld", n=16)
    for ic in range(16):
        t_wvT = prog.emit("sync", lambda e, ic=ic: e.dma_start(
            out=wvT[:, ic * D:(ic + 1) * D], in_=wbf["wv"][:, ic * 128:(ic + 1) * 128], transpose=True),
            waits=([t_wcast["wv"]] if ic == 0 else []), inc="wtld", n=16)
    t_vst1 = prog.emit("vector", lambda e: e.memset(vstage[:, :], 1.0), waits=[t_p1a], inc="vst", n=1)

    t_gk_w = None
    t_gv_w = None
    for pc in range(N_PROD):
        for tg in range(2):
            t0 = pc * CHUNK + tg * 512
            for ic in range(16):
                w0 = []
                if ic == 0:
                    w0 = [prog.tok("pek"), prog.tok("pev")]
                    if pc == 0 and tg == 0:
                        w0 += [t_wkT, t_wvT, t_xn2_done]
                t_fm2 = prog.emit("sync", lambda e, ic=ic, t0=t0: e.dma_start(
                    out=xfm2[:, ic * 512:(ic + 1) * 512],
                    in_=xn2_sp[t0:t0 + 512, ic * 128:(ic + 1) * 128], transpose=True),
                    waits=w0, inc="tld2", n=16)
            for dc in range(16):
                bank = dc % 2
                for ic in range(17):
                    w = [t_fm2, t_bias, evac_bank.get(("ps_k", bank))] if ic == 0 else []
                    if ic < 16:
                        prog.emit("tensor", lambda e, ic=ic, dc=dc, bank=bank: e.matmul(
                            ps_k[:, bank * 512:(bank + 1) * 512],
                            wkT[:, ic * D + dc * 128: ic * D + dc * 128 + 128],
                            xfm2[:, ic * 512:(ic + 1) * 512],
                            start=(ic == 0), stop=False), waits=w)
                    else:
                        prog.emit("tensor", lambda e, dc=dc, bank=bank: e.matmul(
                            ps_k[:, bank * 512:(bank + 1) * 512],
                            bias_bf[:, D + dc * 128: D + (dc + 1) * 128],
                            ones_sb[:, 0:512],
                            start=False, stop=True, skip_group_check=True), inc="pek", n=1)
                t_kmm = prog.tok("pek")
                tev = prog.emit("vector", lambda e, dc=dc, bank=bank: e.tensor_scalar(
                    kstage[:, dc * 512:(dc + 1) * 512],
                    ps_k[:, bank * 512:(bank + 1) * 512], SCALE, None, ALU.mult),
                    waits=[t_kmm, (t_gk_w if dc == 0 else None)], inc="kev", n=1)
                evac_bank[("ps_k", bank)] = tev
            t_kev = evac_bank[("ps_k", 15 % 2)]

            def gk_write(e, pc=pc, tg=tg):
                return e.dma_start(
                    out=pk_raw[pc * 128:(pc + 1) * 128, :].rearrange(
                        "p (h t) -> p h t", t=CHUNK)[:, :, tg * 512:(tg + 1) * 512],
                    in_=kstage.ap().rearrange("p (h t) -> p h t", t=512))
            t_gk_w = prog.emit("gpsimd", gk_write, waits=[t_kev, evac_bank[("ps_k", 0)]], inc="gkw", n=16)

            for tt4 in range(4):
                for dvg in range(4):
                    bank = dvg % 2
                    for ic in range(17):
                        w = [t_fm2, t_vst1, evac_bank.get(("ps_v", bank)),
                             (t_gv_w if (tt4 == 0 and dvg == 0) else None)] if ic == 0 else []
                        if ic < 16:
                            prog.emit("tensor", lambda e, ic=ic, tt4=tt4, dvg=dvg, bank=bank: e.matmul(
                                ps_v[:, bank * 512:(bank + 1) * 512],
                                xfm2[:, ic * 512 + tt4 * 128: ic * 512 + tt4 * 128 + 128],
                                wvT[:, ic * D + dvg * 512: ic * D + (dvg + 1) * 512],
                                start=(ic == 0), stop=False), waits=w)
                        else:
                            prog.emit("tensor", lambda e, dvg=dvg, bank=bank: e.matmul(
                                ps_v[:, bank * 512:(bank + 1) * 512],
                                ones_sb[0:1, 0:128],
                                bias_bf[:, 2 * D + dvg * 512: 2 * D + (dvg + 1) * 512],
                                start=False, stop=True, skip_group_check=True), inc="pev", n=1)
                    t_vmm = prog.tok("pev")
                    tev = prog.emit("vector", lambda e, dvg=dvg, bank=bank: e.tensor_copy(
                        vstage.ap().rearrange("p (h c) -> p h c", c=129)[:, 4 * dvg:4 * dvg + 4, 0:128],
                        ps_v[:, bank * 512:(bank + 1) * 512].rearrange("p (h c) -> p h c", c=128)),
                        waits=[t_vmm, (t_gv_w if dvg == 0 else None)], inc="vev", n=1)
                    evac_bank[("ps_v", bank)] = tev
                t_vev = evac_bank[("ps_v", 3 % 2)]

                def gv_write(e, pc=pc, tg=tg, tt4=tt4):
                    r0 = pc * CHUNK + tg * 512 + tt4 * 128
                    return e.dma_start(out=pv_raw[r0:r0 + 128, :], in_=vstage[:, :])
                t_gv_w = prog.emit("gpsimd", gv_write, waits=[t_vev, evac_bank[("ps_v", 0)]], inc="gvw", n=16)




    maybe_stop(3)
    t_p1b = prog.barrier("p1b")
    sbfree("ps_v", "ps_k", "vstage", "kstage", "xfm2", "wvT", "wkT")

    wqT = sballoc("wqT", [128, 16 * D], BF16)
    xfm3 = sballoc("xfm3", [128, 16 * 512], BF16)
    qtmp = sballoc("qtmp", [128, D], BF16)
    ps_q = psalloc("ps_q", [128, 2 * 512])

    t_p1b_dmas = [prog.tok("gkw"), prog.tok("gvw")]
    for ic in range(16):
        t_wqT = prog.emit("sync", lambda e, ic=ic: e.dma_start(
            out=wqT[:, ic * D:(ic + 1) * D], in_=wbf["wq"][:, ic * 128:(ic + 1) * 128], transpose=True),
            waits=([t_wcast["wq"], t_p1b] + t_p1b_dmas if ic == 0 else []), inc="wtld", n=16)

    t_qsp = None
    for qc in range(NQ):
        for tg in range(2):
            t0 = qc * CHUNK + tg * 512
            for ic in range(16):
                t_fm3 = prog.emit("sync", lambda e, ic=ic, t0=t0: e.dma_start(
                    out=xfm3[:, ic * 512:(ic + 1) * 512],
                    in_=xn2_sp[t0:t0 + 512, ic * 128:(ic + 1) * 128], transpose=True),
                    waits=([t_wqT, prog.tok("peq")] if ic == 0 else []), inc="tld3", n=16)
            for tt4 in range(4):
                for dqg in range(4):
                    bank = dqg % 2
                    for ic in range(17):
                        w = [t_fm3, t_bias, evac_bank.get(("ps_q", bank)),
                             (t_qsp if (tt4 == 0 and dqg == 0) else None)] if ic == 0 else []
                        if ic < 16:
                            prog.emit("tensor", lambda e, ic=ic, tt4=tt4, dqg=dqg, bank=bank: e.matmul(
                                ps_q[:, bank * 512:(bank + 1) * 512],
                                xfm3[:, ic * 512 + tt4 * 128: ic * 512 + tt4 * 128 + 128],
                                wqT[:, ic * D + dqg * 512: ic * D + (dqg + 1) * 512],
                                start=(ic == 0), stop=False), waits=w)
                        else:
                            prog.emit("tensor", lambda e, dqg=dqg, bank=bank: e.matmul(
                                ps_q[:, bank * 512:(bank + 1) * 512],
                                ones_sb[0:1, 0:128],
                                bias_bf[:, dqg * 512:(dqg + 1) * 512],
                                start=False, stop=True, skip_group_check=True), inc="peq", n=1)
                    t_qmm = prog.tok("peq")
                    tev = prog.emit("vector", lambda e, dqg=dqg, bank=bank: e.tensor_scalar(
                        qtmp[:, dqg * 512:(dqg + 1) * 512],
                        ps_q[:, bank * 512:(bank + 1) * 512], SCALE, None, ALU.mult),
                        waits=[t_qmm], inc="qev", n=1)
                    evac_bank[("ps_q", bank)] = tev
                row0 = t0 + tt4 * 128
                t_qsp = prog.emit("gpsimd", lambda e, row0=row0: e.dma_start(
                    out=q_sp[row0:row0 + 128, :], in_=qtmp[:, :]),
                    waits=[evac_bank[("ps_q", 1)]], inc="qsp", n=16)


    maybe_stop(4)
    t_p1c = prog.barrier("p1c")
    sbfree("ps_q", "qtmp", "xfm3", "wqT")

    from concourse.bass import IndirectOffsetOnAxis as IOA
    pstg = sballoc("pstg", [128, 2 * 4096], BF16)
    t_pg_q = {}
    t_pkv_done = [prog.tok("qsp"), t_p1c, t_itab]
    t_stg = {0: None, 1: None}
    q_sup = q_sp.rearrange("r c -> (r c)").rearrange("(n w) -> n w", w=4096)
    for v in range(NVIS if PG_ON[2] else 0):
        for j in range(4):
            if not (PG_SKIP <= v * 4 + j < PG_SKIP + PG_N):
                continue
            sb = (v * 4 + j) % 2
            tg1 = prog.emit("gpsimd", lambda e, v=v, j=j, sb=sb: e.indirect_dma_start(
                out=pstg[:, sb * 4096:(sb + 1) * 4096], out_offset=None,
                in_=q_sup, in_offset=IOA(itab_sb[:, NVIS + v * 4 + j:NVIS + v * 4 + j + 1], 0)),
                waits=t_pkv_done + [t_stg[0], t_stg[1]], inc="pgg", n=16)
            t_pg_q[v] = prog.emit("gpsimd", lambda e, v=v, j=j, sb=sb: e.dma_start(
                out=pf_q[v * CHUNK + j * 256: v * CHUNK + (j + 1) * 256, :].rearrange(
                    "(p t) c -> p t c", t=2),
                in_=pstg[:, sb * 4096:(sb + 1) * 4096].rearrange("p (t c) -> p t c", c=D)),
                waits=[tg1], inc="pgq", n=16)
            t_stg[sb] = t_pg_q[v]


    t_pg_done = [prog.tok("pgg"), prog.tok("pgq")]
    sbfree("pstg")

    maybe_stop(5)
    kbuf = sballoc("kbuf", [128, 2 * CHUNK], BF16)
    qbuf = sballoc("qbuf", [128, 2 * CHUNK], BF16)
    vbuf = sballoc("vbuf", [128, 2 * 8 * 129], BF16)
    pbuf = sballoc("pbuf", [128, 3 * CHUNK], BF16)
    dscr = sballoc("dscr", [128, 2 * 1032], F32)
    sb_acc = sballoc("sb_acc", [128, NQ * 1032], F32)
    attn_sb = sballoc("attn_sb", [128, 16 * D], BF16)
    recip = sballoc("recip", [128, 1], F32)
    ps_st = psalloc("ps_st", [128, 2 * CHUNK])
    ps_acc = psalloc("ps_acc", [128, 3 * 512])

    ACCOFF = [512 * (qt // 3) + 129 * (qt % 3) for qt in range(8)]

    zs = sballoc("zs", [1, 512], BF16)
    t_zs = prog.emit("vector", lambda e: e.memset(zs[:, :], 0.0), waits=[t_p1c] + t_pg_done, inc="fvv", n=1)
    for b in range(3):
        prog.emit("tensor", lambda e, b=b: e.matmul(
            ps_acc[:, b * 512:(b + 1) * 512], ones_sb[0:1, 0:128], zs[0:1, 0:512],
            start=True, stop=True, skip_group_check=True),
            waits=([t_zs] if b == 0 else []), inc="peav", n=1)
    for b in range(4):
        prog.emit("tensor", lambda e, b=b: e.matmul(
            ps_st[:, b * 512:(b + 1) * 512], ones_sb[0:1, 0:128], zs[0:1, 0:512],
            start=True, stop=True, skip_group_check=True), inc="pesc", n=1)

    def kv_load(h, v, parity, wk_, wq_, wv_):
        tk = prog.emit("sync", lambda e, h=h, v=v, parity=parity: e.dma_start(
            out=kbuf[:, parity * CHUNK:(parity + 1) * CHUNK],
            in_=pk_raw[v * 128:(v + 1) * 128, h * CHUNK:(h + 1) * CHUNK]),
            waits=wk_, inc="fkld", n=16)
        tq = prog.emit("sync", lambda e, h=h, v=v, parity=parity: e.dma_start(
            out=qbuf[:, parity * CHUNK:(parity + 1) * CHUNK],
            in_=pf_q[v * CHUNK:(v + 1) * CHUNK, h * 128:(h + 1) * 128], transpose=True),
            waits=wq_ + [t_pg_q.get(v)], inc="fqld", n=16)
        tv = prog.emit("gpsimd", lambda e, h=h, v=v, parity=parity: e.dma_start(
            out=vbuf.ap()[:, parity * 8 * 129:(parity + 1) * 8 * 129].rearrange("p (kt c) -> p kt c", c=129),
            in_=pv_raw[v * CHUNK:(v + 1) * CHUNK, h * 129:(h + 1) * 129].rearrange("(kt p) c -> p kt c", p=128)),
            waits=wv_, inc="fvld", n=16)
        return tk, tq, tv

    t_k = {}
    t_q = {}
    t_v = {}
    t_exp_by_st = {0: None, 1: None}
    t_pbuf_freed = {}
    t_sc_last = {}
    t_av_last_g = {}
    t_attn_last = None
    t_rb = None
    t_facc = {}

    t_prod_dmas = [prog.tok("gkw"), prog.tok("gvw")]
    t_k[0], t_q[0], t_v[0] = kv_load(0, 0, 0, [t_p1b] + t_prod_dmas + t_pg_done,
                                     [t_p1b] + t_pg_done, [t_p1b] + t_prod_dmas)
    if _DEBUG:
        prog.emit("gpsimd", lambda e: e.dma_start(
            out=dbg["vb"][:, :], in_=vbuf[:, 0:1032]), waits=[t_v[0]], inc="dbgd", n=16)

    for h in range(H_FLASH):
        t_drain = None
        for v in range(NVIS):
            g = h * NVIS + v
            par = g % 2
            if g + 1 < H_FLASH * NVIS:
                nh, nv = (h, v + 1) if v + 1 < NVIS else (h + 1, 0)
                wpg = t_pg_done if g == 0 else []
                t_k[g + 1], t_q[g + 1], t_v[g + 1] = kv_load(
                    nh, nv, 1 - par,
                    [t_sc_last.get(g - 1)] + wpg, [t_sc_last.get(g - 1)] + wpg,
                    [t_av_last_g.get(g - 1)])
            diag = v < 2
            t_av_last = None
            for kt in range(8):
                stb = kt % 2
                q0 = kt * 128 if diag else 0

                segs = [(q0, 512), (512, 1024)] if q0 < 512 else [(q0, 1024)]
                for si, seg in enumerate(segs):
                    a, b_ = seg
                    t_sc = prog.emit("tensor", lambda e, a=a, b_=b_, kt=kt, par=par, stb=stb: e.matmul(
                        ps_st[:, stb * CHUNK + a: stb * CHUNK + b_],
                        kbuf[:, par * CHUNK + kt * 128: par * CHUNK + (kt + 1) * 128],
                        qbuf[:, par * CHUNK + a: par * CHUNK + b_],
                        start=True, stop=True),
                        waits=([t_k[g], t_q[g], t_exp_by_st[stb]] if si == 0 else []),
                        inc=("pesc" if si == len(segs) - 1 else None),
                        n=(1 if si == len(segs) - 1 else None))
                pb = (g * 8 + kt) % 3
                width = CHUNK - q0
                t_e = prog.emit("scalar", lambda e, stb=stb, q0=q0, pb=pb, width=width: e.activation(
                    pbuf[:, pb * CHUNK + q0: pb * CHUNK + q0 + width],
                    ps_st[:, stb * CHUNK + q0: stb * CHUNK + q0 + width],
                    AF.Exp, bias=ccol(C_SHIFT)),
                    waits=[t_sc, t_pbuf_freed.get(pb)], inc="fex", n=1)
                t_exp_by_st[stb] = t_e
                if diag:
                    t_e = prog.emit("vector", lambda e, pb=pb, q0=q0: e.tensor_tensor(
                        pbuf[:, pb * CHUNK + q0: pb * CHUNK + q0 + 128],
                        pbuf[:, pb * CHUNK + q0: pb * CHUNK + q0 + 128],
                        tri_sb[:, :], ALU.mult),
                        waits=[t_e, t_tri], inc="fvv", n=1)
                qsubs = list(range(kt, 8)) if diag else list(range(8))
                for j, qsub in enumerate(qsubs):
                    first = (kt == 0 and qsub in (0, 3, 6))
                    last = (kt == 7) or (diag and kt == qsub)
                    w = [t_e, t_v[g]]
                    if kt == 0 and j == 0:
                        w.append(t_drain)
                    t_av_last = prog.emit("tensor", lambda e, pb=pb, qsub=qsub, par=par, kt=kt, first=first, last=last: e.matmul(
                        ps_acc[:, ACCOFF[qsub]: ACCOFF[qsub] + 129],
                        pbuf[:, pb * CHUNK + qsub * 128: pb * CHUNK + (qsub + 1) * 128],
                        vbuf[:, par * 8 * 129 + kt * 129: par * 8 * 129 + (kt + 1) * 129],
                        start=first, stop=last, skip_group_check=True),
                        waits=w, inc="peav", n=1)
                t_pbuf_freed[pb] = t_av_last
            t_sc_last[g] = prog.tok("pesc")
            t_av_last_g[g] = t_av_last
            if _DEBUG and g == 0:
                for pbi in range(3):
                    t_pbuf_freed[pbi] = prog.emit("gpsimd", lambda e, pbi=pbi: e.dma_start(
                        out=dbg["pb"][:, pbi * CHUNK:(pbi + 1) * CHUNK],
                        in_=pbuf[:, pbi * CHUNK:(pbi + 1) * CHUNK]),
                        waits=[t_pbuf_freed.get(pbi)], inc="dbgd", n=16)
                t_kq = prog.emit("gpsimd", lambda e: e.dma_start(
                    out=dbg["kq"][:, 0:CHUNK], in_=kbuf[:, 0:CHUNK]),
                    waits=[t_sc_last[0]], inc="dbgd", n=16)
                t_sc_last[0] = prog.emit("gpsimd", lambda e: e.dma_start(
                    out=dbg["kq"][:, CHUNK:2 * CHUNK], in_=qbuf[:, 0:CHUNK]),
                    waits=[t_kq], inc="dbgd", n=16)
            dpar = g % 2
            for bk in range(3):
                lo = bk * 512
                wdt = 387 if bk < 2 else 258
                t_drain = prog.emit("vector", lambda e, lo=lo, wdt=wdt, dpar=dpar, bk=bk: e.tensor_copy(
                    dscr[:, dpar * 1032 + bk * 387: dpar * 1032 + bk * 387 + wdt],
                    ps_acc[:, lo: lo + wdt]),
                    waits=[t_av_last, (t_facc.get(dpar) if bk == 0 else None)], inc="fvv", n=1)

            from concourse.bass import IndirectOffsetOnAxis as IOA

            def facc(e, v=v, dpar=dpar):
                return e.indirect_dma_start(
                    out=dacc[:, :],
                    out_offset=IOA(itab_sb[:, v:v + 1], 0),
                    in_=dscr[:, dpar * 1032:(dpar + 1) * 1032],
                    in_offset=None,
                    compute_op=(ALU.bypass if v < 2 else ALU.add))
            t_facc[dpar] = prog.emit("gpsimd", facc,
                                     waits=[t_drain, t_itab, (t_rb if v < 2 else None)], inc="facc", n=16)
            if _DEBUG and h == 0 and v == 0:
                t_facc[dpar] = prog.emit("gpsimd", lambda e, dpar=dpar: e.dma_start(
                    out=dbg["ds"][:, :], in_=dscr[:, dpar * 1032:(dpar + 1) * 1032]),
                    waits=[t_facc[dpar]], inc="dbgd", n=16)
        t_rb = prog.emit("sync", lambda e: e.dma_start(
            out=sb_acc.ap().rearrange("p (c w) -> p c w", w=1032),
            in_=dacc.rearrange("(c p) w -> p c w", p=128)),
            waits=[t_facc[0], t_facc[1], t_attn_last], inc="frb", n=16)
        if _DEBUG and h == 0:
            prog.emit("gpsimd", lambda e: e.dma_start(
                out=dbg["vv"].rearrange("p (c w) -> p c w", w=1032),
                in_=dacc.rearrange("(c p) w -> p c w", p=128)),
                waits=[t_rb], inc="dbgd", n=16)
        if _DEBUG and h == 1:
            prog.emit("gpsimd", lambda e: e.dma_start(
                out=dbg["kk"][:, 0:1032], in_=dacc[0:128, :]),
                waits=[t_rb], inc="dbgd", n=16)
        for qt16 in range(16):
            qc, qt = qt16 // 8, qt16 % 8
            base = qc * 1032 + qt * 129
            t_rc = prog.emit("vector", lambda e, base=base: e.reciprocal(
                recip[:, :], sb_acc[:, base + 128: base + 129]), waits=[t_rb], inc="fvv", n=1)
            t_attn_last = prog.emit("vector", lambda e, base=base, qt16=qt16, h=h: e.tensor_scalar(
                attn_sb[:, qt16 * D + h * 128: qt16 * D + (h + 1) * 128],
                sb_acc[:, base: base + 128], recip[:, :], None, ALU.mult),
                waits=[t_rc], inc="fvv", n=1)

    t_atsp = None
    for qt16 in range(16):
        t_atsp = prog.emit("gpsimd", lambda e, qt16=qt16: e.dma_start(
            out=attn_sp[qt16 * 128:(qt16 + 1) * 128, :], in_=attn_sb[:, qt16 * D:(qt16 + 1) * D]),
            waits=([t_attn_last] if qt16 == 0 else []), inc="atsp", n=16)
    t_atsp = prog.tok("atsp")
    maybe_stop(6)
    t_p2 = prog.barrier("p2")
    sbfree("zs", "ps_acc", "ps_st", "recip", "attn_sb", "sb_acc", "dscr", "pbuf", "vbuf", "qbuf", "kbuf")

    woT = sballoc("woT", [128, 16 * D], BF16)
    afm = sballoc("afm", [128, 16 * 512], BF16)
    x2t = sballoc("x2t", [128, D], F32)
    xr = sballoc("xr", [128, D], F32)
    xj = sballoc("xj", [128, D], F32)
    xaff3 = sballoc("xaff3", [128, D], BF16)
    ps_o = psalloc("ps_o", [128, 2 * 512])

    for ic in range(16):
        t_woT = prog.emit("sync", lambda e, ic=ic: e.dma_start(
            out=woT[:, ic * D:(ic + 1) * D], in_=wbf["wo"][:, ic * 128:(ic + 1) * 128], transpose=True),
            waits=([t_wcast["wo"], t_p2, t_atsp, prog.tok("fkld"), prog.tok("fvld")] if ic == 0 else []),
            inc="wtld", n=16)

    t_aff2_rd = None
    for qc in range(NQ):
        for tg in range(2):
            t0g = qc * CHUNK + tg * 512
            for ic in range(16):
                w0 = []
                if ic == 0:
                    w0 = [prog.tok("peo")] if prog.count.get("peo") else []
                    if qc == 0 and tg == 0:
                        w0 += [t_atsp, t_woT]
                t_afm = prog.emit("sync", lambda e, ic=ic, t0g=t0g: e.dma_start(
                    out=afm[:, ic * 512:(ic + 1) * 512],
                    in_=attn_sp[t0g:t0g + 512, ic * 128:(ic + 1) * 128], transpose=True),
                    waits=w0, inc="tld4", n=16)
            for tt4 in range(4):
                row0 = t0g + tt4 * 128
                tx1 = prog.emit("gpsimd", lambda e, row0=row0: e.dma_start(
                    out=xr[:, :], in_=x1_sp[row0:row0 + 128, :]),
                    waits=[evac_bank.get(("xr", 0)), evac_bank.get(("x2sp", 0))], inc="x1ld", n=16)
                for og in range(4):
                    bank = og % 2
                    for ic in range(17):
                        w = [t_afm, evac_bank.get(("ps_o", bank))] if ic == 0 else []
                        if ic < 16:
                            prog.emit("tensor", lambda e, ic=ic, tt4=tt4, og=og, bank=bank: e.matmul(
                                ps_o[:, bank * 512:(bank + 1) * 512],
                                afm[:, ic * 512 + tt4 * 128: ic * 512 + tt4 * 128 + 128],
                                woT[:, ic * D + og * 512: ic * D + (og + 1) * 512],
                                start=(ic == 0), stop=False), waits=w)
                        else:
                            prog.emit("tensor", lambda e, og=og, bank=bank: e.matmul(
                                ps_o[:, bank * 512:(bank + 1) * 512],
                                ones_sb[0:1, 0:128],
                                bias_bf[:, 3 * D + og * 512: 3 * D + (og + 1) * 512],
                                start=False, stop=True, skip_group_check=True), inc="peo", n=1)
                    t_omm = prog.tok("peo")
                    tev = prog.emit("vector", lambda e, og=og, bank=bank: e.tensor_tensor(
                        x2t[:, og * 512:(og + 1) * 512], ps_o[:, bank * 512:(bank + 1) * 512],
                        xr[:, og * 512:(og + 1) * 512], ALU.add),
                        waits=[t_omm, tx1], inc="oev", n=1)
                    evac_bank[("ps_o", bank)] = tev
                t_x2 = evac_bank[("ps_o", 1)]
                evac_bank[("xr", 0)] = t_x2
                t_x2sp = prog.emit("gpsimd", lambda e, row0=row0: e.dma_start(
                    out=x2_sp[row0:row0 + 128, :], in_=x2t[:, :]), waits=[t_x2], inc="x2sp", n=16)
                evac_bank[("x2sp", 0)] = t_x2sp
                tst4 = emit_ln("ln4", x2t[:, :], xj[:, :], 24, t_x2)
                ta1 = ln_apply(xj[:, :], x2t[:, :], 24, tst4)
                tst5 = emit_ln("ln5", xj[:, :], x2t[:, :], 32, ta1, junk_guard=t_x2sp)
                ta2 = ln_apply(xj[:, :], xj[:, :], 32, tst5)
                evac_bank[("x2t", 0)] = ta2
                ta3 = prog.emit("vector", lambda e: e.tensor_tensor(xj[:, :], xj[:, :], lnw_b[:, :], ALU.mult),
                                waits=[ta2], inc="lnv", n=1)
                ta4 = prog.emit("vector", lambda e: e.tensor_tensor(xaff3[:, :], xj[:, :], lnb_b[:, :], ALU.add),
                                waits=[ta3, t_aff2_rd], inc="lnv", n=1)
                t_aff2_rd = prog.emit("gpsimd", lambda e, row0=row0: e.dma_start(
                    out=xaff2_sp[row0:row0 + 128, :], in_=xaff3[:, :]),
                    waits=[ta4], inc="af2sp", n=16)
    t_af2 = prog.tok("af2sp")

    maybe_stop(7)
    t_p3a = prog.barrier("p3a")
    sbfree("ps_o", "xaff3", "xj", "xr", "x2t", "afm", "woT")

    wp2 = sballoc("wp2", [128, 16 * D], BF16)
    xfm4 = sballoc("xfm4", [128, 16 * 512], BF16)
    x3t = sballoc("x3t", [128, D], F32)
    ps_g = psalloc("ps_g", [128, 2 * 512])

    t_p3a_dmas = [prog.tok("af2sp"), prog.tok("x2sp"), prog.tok("x1ld")]
    for ic in range(16):
        t_wp2 = prog.emit("sync", lambda e, ic=ic: e.dma_start(
            out=wp2[:, ic * D:(ic + 1) * D], in_=wp_sp[ic * 128:(ic + 1) * 128, :]),
            waits=([t_wp_sp[-1], t_p3a] + t_p3a_dmas if ic == 0 else []), inc="wtld", n=16)

    t_out = None
    for qc in range(NQ):
        for tg in range(2):
            t0g = qc * CHUNK + tg * 512
            for ic in range(16):
                w0 = []
                if ic == 0:
                    w0 = [prog.tok("pf2")] if prog.count.get("pf2") else []
                    if qc == 0 and tg == 0:
                        w0 += [t_af2, t_wp2]
                t_fm4 = prog.emit("sync", lambda e, ic=ic, t0g=t0g: e.dma_start(
                    out=xfm4[:, ic * 512:(ic + 1) * 512],
                    in_=xaff2_sp[t0g:t0g + 512, ic * 128:(ic + 1) * 128], transpose=True),
                    waits=w0, inc="tld5", n=16)
            for tt4 in range(4):
                row0 = t0g + tt4 * 128
                tx2 = prog.emit("gpsimd", lambda e, row0=row0: e.dma_start(
                    out=x3t[:, :], in_=x2_sp[row0:row0 + 128, :]),
                    waits=[prog.tok("x2sp"), (t_out if t_out else None)], inc="x2ld", n=16)
                for og in range(4):
                    bank = og % 2
                    for ic in range(16):
                        w = [t_fm4, evac_bank.get(("ps_g", bank))] if ic == 0 else []
                        prog.emit("tensor", lambda e, ic=ic, tt4=tt4, og=og, bank=bank: e.matmul(
                            ps_g[:, bank * 512:(bank + 1) * 512],
                            xfm4[:, ic * 512 + tt4 * 128: ic * 512 + tt4 * 128 + 128],
                            wp2[:, ic * D + og * 512: ic * D + (og + 1) * 512],
                            start=(ic == 0), stop=(ic == 15)),
                            waits=w, inc=("pf2" if ic == 15 else None), n=(1 if ic == 15 else None))
                    t_f2 = prog.tok("pf2")
                    tev = prog.emit("vector", lambda e, og=og, bank=bank: e.tensor_tensor(
                        x3t[:, og * 512:(og + 1) * 512], ps_g[:, bank * 512:(bank + 1) * 512],
                        x3t[:, og * 512:(og + 1) * 512], ALU.add),
                        waits=[t_f2, tx2], inc="f2ev", n=1)
                    evac_bank[("ps_g", bank)] = tev
                t_x3 = evac_bank[("ps_g", 1)]
                t_out = prog.emit("scalar", lambda e, row0=row0: e.dma_start(
                    out=xout[row0:row0 + 128, :], in_=x3t[:, :]), waits=[t_x3], inc="outd", n=16)

    prog.emit("gpsimd", lambda e: None, waits=[prog.tok("outd")])
    if _DEBUG:
        wsp = [prog.tok("x1sp"), prog.tok("xn2sp"), prog.tok("atsp"),
               prog.tok("x2sp"), prog.tok("outd")]
        prog.emit("gpsimd", lambda e: e.dma_start(out=dbg["x1"][:, :], in_=x1_sp[0:CHUNK, :]),
                  waits=wsp, inc="dbgd", n=16)
        prog.emit("gpsimd", lambda e: e.dma_start(out=dbg["xn2"][:, :], in_=xn2_sp[0:CHUNK, :]),
                  inc="dbgd", n=16)
        prog.emit("gpsimd", lambda e: e.dma_start(out=dbg["attn"][:, :], in_=attn_sp[0:CHUNK, :]),
                  inc="dbgd", n=16)
        prog.emit("gpsimd", lambda e: e.dma_start(out=dbg["x2"][:, :], in_=x2_sp[0:128, :]),
                  inc="dbgd", n=16)
        prog.emit("gpsimd", lambda e: e.dma_start(out=dbg["kd"][:, :], in_=pk_raw[0:128, 0:CHUNK]),
                  inc="dbgd", n=16)
        prog.emit("gpsimd", lambda e: e.dma_start(out=dbg["wp"][:, :], in_=wp_sp[:, :]),
                  inc="dbgd", n=16)
        prog.emit("gpsimd", lambda e: e.dma_start(out=dbg["qd"][:, :], in_=pf_q[0:CHUNK, 0:128]),
                  inc="dbgd", n=16)
        prog.emit("gpsimd", lambda e: None, waits=[prog.tok("dbgd")])

    from contextlib import ExitStack
    stack = ExitStack()
    sems = {name: stack.enter_context(nc.semaphore(name)) for name in prog.sem_names}
    block = stack.enter_context(nc.Block())
    block.sync(_mk_body(prog, sems, "sync"))
    block.scalar(_mk_body(prog, sems, "scalar"))
    block.vector(_mk_body(prog, sems, "vector"))
    block.tensor(_mk_body(prog, sems, "tensor"))
    block.gpsimd(_mk_body(prog, sems, "gpsimd"))
    stack.close()
    while guards:
        guards.pop()[1].__exit__(None, None, None)
    return nc



def kernel(L0_attn_X, L0_attn_Wq, L0_attn_bq, L0_attn_Wk, L0_attn_bk,
           L0_attn_Wv, L0_attn_bv, L0_attn_Wo, L0_attn_bo,
           L0_ffn_W, L0_ffn_ln_w, L0_ffn_ln_b):
    X = np.ascontiguousarray(np.asarray(L0_attn_X, dtype=np.float32))
    biases = np.ascontiguousarray(
        np.concatenate([np.asarray(b, dtype=np.float32).ravel()
                        for b in (L0_attn_bq, L0_attn_bk, L0_attn_bv, L0_attn_bo)]))
    lnwb = np.ascontiguousarray(np.stack([np.asarray(L0_ffn_ln_w, dtype=np.float32),
                                          np.asarray(L0_ffn_ln_b, dtype=np.float32)]))

    nc = build_nc()
    in_maps = []
    for c in range(NCHIP):
        b, qa, qb, prod, visits = _core_plan(c)
        xin = np.concatenate([X[b, s * CHUNK:(s + 1) * CHUNK, :] for s in prod], axis=0)
        b_, qa_, qb_, prod_, visits_ = _core_plan(c)
        itab = np.zeros((128, 5 * NVIS), dtype=np.int32)
        ar = np.arange(128)
        for v, (qc_, s_) in enumerate(visits_):
            itab[:, v] = qc_ * 128 + ar
            for j in range(4):
                itab[:, NVIS + v * 4 + j] = qc_ * 512 + j * 128 + ar
        if PG_IDX_OVERRIDE is not None:
            itab[:, NVIS] = PG_IDX_OVERRIDE + ar
        in_maps.append({
            "xin": np.ascontiguousarray(xin),
            "tab": _make_tab(c),
            "itab": np.ascontiguousarray(itab),
            "wq": np.ascontiguousarray(L0_attn_Wq, dtype=np.float32),
            "wk": np.ascontiguousarray(L0_attn_Wk, dtype=np.float32),
            "wv": np.ascontiguousarray(L0_attn_Wv, dtype=np.float32),
            "wo": np.ascontiguousarray(L0_attn_Wo, dtype=np.float32),
            "wf": np.ascontiguousarray(L0_ffn_W, dtype=np.float32),
            "biases": biases,
            "lnwb": lnwb,
        })
    res = run_bass_kernel_spmd(nc, in_maps, core_ids=list(range(NCHIP)))

    out = np.empty((2, 8192, D), dtype=np.float32)
    for c in range(NCHIP):
        b, qa, qb, prod, visits = _core_plan(c)
        r = res.results[c]["xout"]
        out[b, qa * CHUNK:(qa + 1) * CHUNK, :] = r[0:CHUNK]
        out[b, qb * CHUNK:(qb + 1) * CHUNK, :] = r[CHUNK:2 * CHUNK]
    if _DEBUG:
        kernel._dbg = res.results
    return out
